# version 01 (final)
# baseline (speedup 1.0000x reference)
import contextlib
import math

import numpy as np
import ml_dtypes

import concourse.bass as bass
import concourse.mybir as mybir
from concourse.bass_utils import run_bass_kernel_spmd

F32 = mybir.dt.float32
BF16 = mybir.dt.bfloat16
AX = mybir.AxisListType.X
ALU = mybir.AluOpType
ACT = mybir.ActivationFunctionType

D = 1024
T = 2048
NB = 2
S = NB * T
NCH = S // 128
TCH = T // 128
HID = 4096
NSL = 4096
QPI = math.pi / 4
HPI = math.pi / 2

CONSTS = [0.0, 1.0, QPI, HPI, 0.25, -0.25]

ENGS = ("pe", "dve", "act", "sp", "gp")
QCH = tuple(f"q{i}" for i in range(8))
GCH = tuple(f"g{i}" for i in range(8))
SEMS = ("pe", "dve", "act") + QCH + GCH


class KB:
    def __init__(self):
        self.nc = bass.Bass("TRN2", target_bir_lowering=False)
        self.em = {e: [] for e in ENGS}
        self.cnt = {s: 0 for s in SEMS}
        self.last_wait = {e: {} for e in ENGS}
        self.sems = {}

    def op(self, eng, fn, waits=(), inc=None, amt=1):
        lw = self.last_wait[eng]
        fw = tuple((s, v) for (s, v) in waits if v and v > lw.get(s, 0))
        for s, v in fw:
            lw[s] = v
        after = None
        if inc is not None:
            self.cnt[inc] += amt
            after = self.cnt[inc]
        sems = self.sems

        def wrapped(h, fw=fw, fn=fn, inc=inc, amt=amt):
            for s, v in fw:
                h.wait_ge(sems[s], v)
            ins = fn(h)
            if inc is not None:
                ins.then_inc(sems[inc], amt)
            return ins

        self.em[eng].append(wrapped)
        return after

    def v(self, sem):
        return self.cnt[sem]

    def barrier_waits(self):
        return [(s, self.cnt[s]) for s in SEMS if s != "sp" and self.cnt[s]]

    def dma_in(self, dst, src, waits=(), ch="q0"):
        prev = self.cnt[ch]
        w = list(waits) + ([(ch, prev)] if prev else [])
        return ch, self.op("sp", lambda e: e.dma_start(out=dst, in_=src),
                           w, ch, 16)

    def dma_out(self, dst, src, waits=(), ch="g0"):
        prev = self.cnt[ch]
        w = list(waits) + ([(ch, prev)] if prev else [])
        return ch, self.op("gp", lambda e: e.dma_start(out=dst, in_=src),
                           w, ch, 16)

    def mm(self, out, lhsT, rhs, start, stop, waits=(), inc=False):
        return self.op("pe", lambda e: e.matmul(out, lhsT, rhs, start=start,
                                                stop=stop),
                       waits, "pe" if inc else None)

    def tr(self, out, in_, ident, waits=(), inc=False):
        return self.op("pe", lambda e: e.transpose(out, in_, ident),
                       waits, "pe" if inc else None)

    def cp(self, eng, out, in_, waits=(), inc=False):
        if eng == "act":
            return self.op("act", lambda e: e.copy(out=out, in_=in_),
                           waits, "act" if inc else None)
        return self.op("dve", lambda e: e.tensor_copy(out=out, in_=in_),
                       waits, "dve" if inc else None)

    def ts(self, out, in0, s1, s2=None, op0=ALU.mult, op1=ALU.add, waits=(),
           inc=False, accum=None):
        if s2 is None:
            return self.op("dve", lambda e: e.tensor_scalar(
                out=out, in0=in0, scalar1=s1, scalar2=None, op0=op0,
                accum_out=accum), waits, "dve" if inc else None)
        return self.op("dve", lambda e: e.tensor_scalar(
            out=out, in0=in0, scalar1=s1, scalar2=s2, op0=op0, op1=op1,
            accum_out=accum), waits, "dve" if inc else None)

    def tt(self, out, in0, in1, alu=ALU.mult, waits=(), inc=False):
        return self.op("dve", lambda e: e.tensor_tensor(
            out=out, in0=in0, in1=in1, op=alu), waits, "dve" if inc else None)

    def stt(self, out, in0, scalar, in1, op0=ALU.mult, op1=ALU.add, waits=(),
            inc=False, accum=None):
        return self.op("dve", lambda e: e.scalar_tensor_tensor(
            out=out, in0=in0, scalar=scalar, in1=in1, op0=op0, op1=op1,
            accum_out=accum), waits, "dve" if inc else None)

    def actv(self, out, in_, func, bias=0.0, scale=1.0, waits=(), inc=False,
             accum=None):
        return self.op("act", lambda e: e.activation(
            out=out, in_=in_, func=func, bias=bias, scale=scale,
            accum_out=accum), waits, "act" if inc else None)

    def red(self, out, in_, rop=ALU.max, waits=(), inc=False):
        return self.op("dve", lambda e: e.tensor_reduce(
            out=out, in_=in_, axis=AX, op=rop), waits, "dve" if inc else None)

    def recip(self, out, in_, waits=(), inc=False):
        return self.op("dve", lambda e: e.reciprocal(out=out, in_=in_),
                       waits, "dve" if inc else None)


def hbm_blk(ap2d, p=128):
    return ap2d.rearrange("(c p) n -> p c n", p=p)


_PHASE_SNAPS = {}


def _snap_phase(o, k):
    _PHASE_SNAPS[k] = ({e: len(o.em[e]) for e in ENGS},
                       dict(o.cnt))


def build(debug=False, upto=99):
    _PHASE_SNAPS.clear()
    o = KB()
    nc = o.nc

    def din(name, shape, dt=F32):
        return nc.dram_tensor(name, shape, dt, kind="ExternalInput").ap()

    def dscr(name, shape, dt=F32):
        kind = "ExternalOutput" if debug else "Internal"
        return nc.dram_tensor(name, shape, dt, kind=kind).ap()

    x_d = din("x", [S, D])
    wqk_d = din("wqk", [D, 32])
    wv_d = din("wv", [D, D])
    wo_d = din("wo", [D, D])
    w1a_d = din("w1a", [D, HID])
    w2a_d = din("w2a", [HID, D])
    w1f_d = din("w1f", [D, HID])
    w2f_d = din("w2f", [HID, D])
    se_d = din("se", [NSL, D])
    vecs_d = din("vecs_c", [128, 56])
    b1a_d = din("b1a", [1, HID])
    b2r_d = din("b2row", [1, D])
    consts_d = din("consts_c", [128, len(CONSTS)])
    identf_d = din("identf_c", [128, 128])
    identb_d = din("identb_c", [128, 128], BF16)
    trib_d = din("trib_c", [128, 128], BF16)
    trif_d = din("trif_c", [128, 128])
    ejf_d = din("ejf_c", [128, 31])
    ejb_d = din("ejb_c", [128, 31], BF16)
    u16f_d = din("u16f_c", [16, 16])
    onesrb_d = din("onesrb_c", [1, 128], BF16)
    onesrf_d = din("onesrf_c", [1, 128])
    onescb_d = din("onescb_c", [128, 1], BF16)

    out_d = nc.dram_tensor("out", [S, D], F32, kind="ExternalOutput").ap()

    xn_d = dscr("xn_s", [S, D])
    x1_d = dscr("x1_s", [S, D])
    outp_d = dscr("outp_s", [S, D])
    z3_d = dscr("z3_s", [S, D], BF16)
    z3T_d = dscr("z3T_s", [D, S], BF16)
    z2T_d = dscr("z2T_s", [D, S], BF16)
    seT_d = dscr("seT_s", [D, NSL], BF16)
    t4_d = dscr("t4_s", [NB, NSL, T], BF16)
    slT_d = dscr("slT_s", [NB, D, NSL], BF16)
    p_d = dscr("p_s", [NB, NSL, D], BF16)
    hT_dbg = dscr("hT_dbg", [NB, D, T], BF16) if debug else None
    cr_dbg = dscr("cr_dbg", [34, T], F32) if debug else None
    dbgq = dscr("dbgq", [128, 2048], F32) if debug else None
    dbgm = dscr("dbgm", [128, 2048], BF16) if debug else None
    dbgb = dscr("dbgb", [128, 96], F32) if debug else None
    dbgh = dscr("dbgh", [128, 2048], BF16) if debug else None
    dbgw = dscr("dbgw", [128, 65536], BF16) if debug else None

    es = contextlib.ExitStack()
    with es:
        E = es.enter_context

        def sbt(name, shape, dt=F32):
            return E(nc.sbuf_tensor(name, shape, dt))

        A = sbt("arA", [128, 16384])
        B = sbt("arB", [128, 16384])
        C = sbt("arC", [128, 16384])
        Dn = sbt("arD", [128, 2048])

        consts = sbt("consts", [128, len(CONSTS)])
        identb = sbt("identb", [128, 128], BF16)
        trib = sbt("trib", [128, 128], BF16)
        trif = sbt("trif", [128, 128])
        ejf = sbt("ejf", [128, 31])
        ejb = sbt("ejb", [128, 31], BF16)
        u16f = sbt("u16f", [16, 16])
        onesrb = sbt("onesrb", [1, 128], BF16)
        onesrf = sbt("onesrf", [1, 128])
        onescb = sbt("onescb", [128, 1], BF16)
        vecs = sbt("vecs", [128, 56])
        vecsb = sbt("vecsb", [128, 56], BF16)

        stat = sbt("stat", [128, 2 * NCH])
        rbuf = sbt("rbuf", [128, NCH])
        c3buf = sbt("c3buf", [128, NCH])
        rrbuf = sbt("rrbuf", [128, NCH])
        r2buf = sbt("r2buf", [128, NCH])
        b2rb = sbt("b2rb", [128, NCH])
        st2 = sbt("st2", [128, 12])
        bqkb = sbt("bqkb", [128, 32])
        bvb = sbt("bvb", [128, D])
        b1pp = sbt("b1pp", [128, 32])
        k0r = sbt("k0r", [1, 2 * NB])
        mk0q = sbt("mk0q", [128, NB])
        smt = sbt("smt", [128, 16])
        qkl_t = sbt("qklt", [128, 32])

        PS = [E(nc.psum_tensor(f"ps{i}", [128, 512], F32)) for i in range(8)]
        PSB = [p.ap().bitcast(BF16) for p in PS]

        for s in SEMS:
            o.sems[s] = E(nc.semaphore(s))
        for i, val in enumerate(CONSTS):
            nc.const_aps.register(F32, float(val), consts[:, i:i + 1])

        def bfv(arena, byte_off, rows, cols):
            q = byte_off // 4
            return arena[0:rows, q:q + (cols + 1) // 2].bitcast(BF16)[0:rows, 0:cols]

        def fv(arena, byte_off, rows, cols):
            return arena[0:rows, byte_off // 4:byte_off // 4 + cols]

        for dst, src in [(consts[:], consts_d[:, :]), (identb[:], identb_d[:, :]), (trib[:], trib_d[:, :]),
                         (trif[:], trif_d[:, :]), (ejf[:], ejf_d[:, :]),
                         (ejb[:], ejb_d[:, :]), (u16f[:], u16f_d[:, :]),
                         (onesrb[:], onesrb_d[:, :]), (onesrf[:], onesrf_d[:, :]),
                         (onescb[:], onescb_d[:, :]), (vecs[:], vecs_d[:, :]),]:
            v_const = o.dma_in(dst, src, ch="q7")
        v_vb = o.cp("dve", vecsb[:], vecs[:], waits=[v_const], inc=True)

        def ln1w(dc): return vecs[:, 0 + dc:1 + dc]
        def ln2w(dc): return vecs[:, 16 + dc:17 + dc]
        def gxw(dc): return vecs[:, 40 + dc:41 + dc]
        LN1B, LN2B = 8, 24

        xst = [fv(C, 0, 128, 1024), fv(C, 4096, 128, 1024)]
        z3st = [bfv(C, 8192, 128, 1024), bfv(C, 10240, 128, 1024)]
        ztst = [bfv(C, 12288, 128, 1024), bfv(C, 14336, 128, 1024)]
        bnt = fv(Dn, 0, 128, 12)
        last_z3 = [0, 0]
        z3_cons = [[], []]
        zt_cons = [[], []]
        last_xs = [0, 0]
        for sc in range(NCH):
            sl = sc % 2
            w = ([("dve", last_z3[sl])] if last_z3[sl] else []) + \
                ([last_xs[sl]] if last_xs[sl] else [])
            v_x = o.dma_in(xst[sl][:, :], x_d[sc * 128:(sc + 1) * 128, :], w,
                           ch=f"q{sl}")
            o.op("dve", lambda e, s=sl: e.bn_stats(out=bnt[:, 0:6],
                 in_=xst[s][:, 0:512]), [v_x])
            v_bs = o.op("dve", lambda e, s=sl: e.bn_stats(out=bnt[:, 6:12],
                 in_=xst[s][:, 512:1024]), inc="dve")
            v_ag = o.op("dve", lambda e, c=sc: e.bn_aggr(
                out=stat[:, 2 * c:2 * c + 2], in_=bnt[:, :]),
                [("dve", v_bs)], inc="dve")
            mu = stat[:, 2 * sc:2 * sc + 1]
            var = stat[:, 2 * sc + 1:2 * sc + 2]
            rc = rbuf[:, sc:sc + 1]
            c3c = c3buf[:, sc:sc + 1]
            v1 = o.ts(smt[:, 0:1], var, 1e-5, None, ALU.add,
                      waits=[("dve", v_ag)], inc=True)
            v2 = o.actv(smt[:, 1:2], smt[:, 0:1], ACT.Sqrt,
                        waits=[("dve", v1)], inc=True)
            v_rc = o.recip(rc, smt[:, 1:2], waits=[("act", v2)], inc=True)
            v_b = o.stt(smt[:, 2:3], mu, -1.0, rc, ALU.mult, ALU.mult,
                        waits=[("dve", v_rc)], inc=True)
            v3 = o.stt(smt[:, 3:4], rc, rc, var, ALU.mult, ALU.mult, inc=True)
            v4 = o.actv(smt[:, 4:5], smt[:, 3:4], ACT.Sqrt,
                        waits=[("dve", v3)], inc=True)
            v5 = o.ts(smt[:, 5:6], smt[:, 4:5], 3.125e-14, None, ALU.max,
                      waits=[("act", v4)], inc=True)
            o.recip(c3c, smt[:, 5:6], waits=[("dve", v5)])
            v_xn = o.ts(xst[sl][:, :], xst[sl][:, :], rc, smt[:, 2:3],
                        ALU.mult, ALU.add, waits=[("dve", v_b)], inc=True)
            last_xs[sl] = o.dma_out(xn_d[sc * 128:(sc + 1) * 128, :],
                                    xst[sl][:, :], [("dve", v_xn)],
                                    ch=f"g{sl}")
            v_z3 = o.ts(z3st[sl][:, :], xst[sl][:, :], c3c, None, ALU.mult,
                        waits=z3_cons[sl], inc=True)
            last_z3[sl] = v_z3
            v_zd = o.dma_out(z3_d[sc * 128:(sc + 1) * 128, :], z3st[sl][:, :],
                             [("dve", v_z3)], ch=f"g{2 + sl}")
            pst = PSB[sl]
            wtr = [("dve", v_z3)] + zt_cons[sl]
            for dc in range(8):
                v_tr = o.tr(pst[:, dc * 128:(dc + 1) * 128],
                            z3st[sl][:, dc * 128:(dc + 1) * 128], identb[:],
                            waits=wtr if dc == 0 else (), inc=(dc == 7))
            z3_cons[sl] = [v_zd, ("pe", v_tr)]
            v_ze = o.cp("dve", ztst[sl][:, :], pst[:, :],
                        [("pe", v_tr)] + zt_cons[sl], True)
            v_ztd = o.dma_out(hbm_blk(z3T_d)[:, :, sc * 128:(sc + 1) * 128],
                              ztst[sl][:].rearrange("p (c s) -> p c s", c=8),
                              [("dve", v_ze)], ch=f"g{4 + sl}")
            zt_cons[sl] = [v_ztd]
        var_all = stat[:].rearrange("p (c k) -> p k c", k=2)[:, 1:2, :].squeeze(1)
        vp = fv(Dn, 192, 128, NCH)
        va = o.tt(vp, rbuf[:], rbuf[:], ALU.mult, inc=True)
        vb = o.tt(vp, vp, var_all, ALU.mult, waits=[("dve", va)], inc=True)
        v1 = o.ts(vp, vp, 1e-5, None, ALU.add, waits=[("dve", vb)], inc=True)
        v2 = o.actv(vp, vp, ACT.Sqrt, waits=[("dve", v1)], inc=True)
        vc = o.recip(vp, vp, waits=[("act", v2)], inc=True)
        vd = o.recip(rrbuf[:], c3buf[:], inc=True)
        v_rr = o.tt(rrbuf[:], rrbuf[:], vp, ALU.mult,
                    waits=[("dve", max(vc, vd))], inc=True)

        _snap_phase(o, 1)
        ggt = fv(Dn, 320, 128, 8)
        v_gg = o.tt(ggt, vecs[:, 40:48], vecs[:, 48:56], ALU.mult,
                    waits=[v_const], inc=True)
        bw = o.barrier_waits()
        sqdump = fv(C, 0, 128, 1024)
        sest = [fv(C, 4096, 128, 1024), fv(C, 8192, 128, 1024)]
        sebst = [bfv(C, 12288, 128, 1024), bfv(C, 14336, 128, 1024)]
        setst = [bfv(Dn, 384, 128, 1024), bfv(Dn, 2432, 128, 1024)]
        last_se = [0, 0]
        seb_cons = [[], []]
        set_cons = [[], []]
        for ncv in range(32):
            sl = ncv % 2
            v_l = o.dma_in(sest[sl][:, :], se_d[ncv * 128:(ncv + 1) * 128, :],
                           ([("dve", last_se[sl])] if last_se[sl] else []) +
                           (bw if ncv < 2 else []), ch=f"q{2 + sl}")
            v_ss = o.stt(sqdump[:, :], sest[sl][:, :], 1.0, sest[sl][:, :],
                         ALU.bypass, ALU.mult, waits=[v_l], inc=True,
                         accum=smt[:, 6:7])
            v_n = o.actv(smt[:, 7:8], smt[:, 6:7], ACT.Sqrt,
                         waits=[("dve", v_ss)], inc=True)
            v_m = o.ts(smt[:, 8:9], smt[:, 7:8], 1e-12, None, ALU.max,
                       waits=[("act", v_n)], inc=True)
            v_rc2 = o.recip(smt[:, 9:10], smt[:, 8:9], waits=[("dve", v_m)],
                            inc=True)
            v_seb = o.ts(sebst[sl][:, :], sest[sl][:, :], smt[:, 9:10], 32.0,
                         ALU.mult, ALU.mult,
                         waits=seb_cons[sl] + [("dve", v_rc2)], inc=True)
            last_se[sl] = v_seb
            pst = PSB[sl]
            wtr = [("dve", v_seb)]
            for dc in range(8):
                v_tr = o.tr(pst[:, dc * 128:(dc + 1) * 128],
                            sebst[sl][:, dc * 128:(dc + 1) * 128], identb[:],
                            waits=wtr if dc == 0 else (), inc=(dc == 7))
            seb_cons[sl] = [("pe", v_tr)]
            for dc in range(8):
                v_ev = o.ts(setst[sl][:, dc * 128:(dc + 1) * 128],
                            pst[:, dc * 128:(dc + 1) * 128], ggt[:, dc:dc + 1],
                            None, ALU.mult,
                            waits=([("pe", v_tr), ("dve", v_gg)] + set_cons[sl])
                            if dc == 0 else (), inc=(dc == 7))
            v_sed = o.dma_out(hbm_blk(seT_d)[:, :, ncv * 128:(ncv + 1) * 128],
                              setst[sl][:].rearrange("p (c n) -> p c n", c=8),
                              [("dve", v_ev)], ch=f"g{6 + sl}")
            set_cons[sl] = [v_sed]

        _snap_phase(o, 2)
        bw = o.barrier_waits()
        wqkp = bfv(Dn, 0, 128, 256)
        wqkb = bfv(Dn, 7168, 128, 256)
        wvb = bfv(B, 0, 128, 8192)
        wob = bfv(B, 16384, 128, 8192)
        stg = [fv(C, 0, 128, 1024), fv(C, 4096, 128, 1024)]
        rowf = fv(C, 9216, 1, 512)
        rowb = bfv(C, 11264, 1, 512)
        v_l = o.dma_in(fv(C, 8192, 128, 256), hbm_blk(wqk_d), bw, ch="q4")
        v_cast = 0
        for dc in range(8):
            o.cp("dve", wqkp[:, dc * 32:(dc + 1) * 32],
                 fv(C, 8192, 128, 256)[:, dc * 32:(dc + 1) * 32],
                 waits=[v_l])
            v_cast = o.ts(wqkb[:, dc * 32:(dc + 1) * 32],
                          fv(C, 8192, 128, 256)[:, dc * 32:(dc + 1) * 32],
                          ln1w(dc), None, ALU.mult, inc=(dc == 7))
        for dc in range(8):
            v_mm = o.mm(PS[0][0:1, 0:32], vecsb[:, LN1B + dc:LN1B + dc + 1],
                        wqkp[:, dc * 32:(dc + 1) * 32], dc == 0, dc == 7,
                        waits=[("dve", max(v_cast, v_vb))] if dc == 0 else (),
                        inc=(dc == 7))
        v_r = o.cp("dve", rowf[0:1, 0:32], PS[0][0:1, 0:32], [("pe", v_mm)], True)
        v_rb = o.cp("dve", rowb[0:1, 0:32], rowf[0:1, 0:32],
                    waits=[("dve", v_r)], inc=True)
        v_bc = o.mm(PS[1][:, 0:32], onesrb[:, :], rowb[0:1, 0:32], True, True,
                    waits=[("dve", v_rb)], inc=True)
        v_bqkb = o.cp("dve", bqkb[:, :], PS[1][:, 0:32], [("pe", v_bc)], True)
        if debug:
            o.dma_out(dbgb[:, 0:32], bqkb[:, :], [("dve", v_bqkb)])
        last_stg = [0, 0]
        v_wv_plain = 0
        for dc in range(8):
            sl = dc % 2
            v_l = o.dma_in(stg[sl][:, :], wv_d[dc * 128:(dc + 1) * 128, :],
                           ([("dve", last_stg[sl])] if last_stg[sl] else []) +
                           (bw if dc < 2 else []), ch=f"q{sl}")
            v_c = o.cp("dve", wvb[:, dc * 1024:(dc + 1) * 1024], stg[sl][:, :],
                       waits=[v_l], inc=True)
            last_stg[sl] = v_c
            v_wv_plain = v_c
        v_bvb = 0
        v_r = 0
        for half in range(2):
            for dc in range(8):
                v_mm = o.mm(PS[0][0:1, :], vecsb[:, LN1B + dc:LN1B + dc + 1],
                            wvb[:, dc * 1024 + half * 512:dc * 1024 + half * 512 + 512],
                            dc == 0, dc == 7,
                            waits=[("dve", v_wv_plain if half == 0 else v_r)]
                            if dc == 0 else (), inc=(dc == 7))
            v_r = o.cp("dve", rowf[0:1, :], PS[0][0:1, :], [("pe", v_mm)], True)
            v_rb = o.cp("dve", rowb[0:1, :], rowf[0:1, :],
                        waits=[("dve", v_r)], inc=True)
            v_bc = o.mm(PS[1][:, :], onesrb[:, :], rowb[0:1, :], True, True,
                        waits=[("dve", v_rb)], inc=True)
            v_bvb = o.cp("dve", bvb[:, half * 512:half * 512 + 512], PS[1][:, :],
                         [("pe", v_bc)], True)
        v_wvf = 0
        for dc in range(8):
            v_wvf = o.ts(wvb[:, dc * 1024:(dc + 1) * 1024],
                         wvb[:, dc * 1024:(dc + 1) * 1024], ln1w(dc), None,
                         ALU.mult, waits=[("pe", v_mm)] if dc == 0 else (),
                         inc=(dc == 7))
        v_wo = 0
        for ncv in range(8):
            sl = ncv % 2
            v_l = o.dma_in(stg[sl][:, :], wo_d[ncv * 128:(ncv + 1) * 128, :],
                           [("dve", last_stg[sl])] if last_stg[sl] else (),
                           ch=f"q{sl}")
            v_wo = o.cp("dve", wob[:, ncv * 1024:(ncv + 1) * 1024], stg[sl][:, :],
                        waits=[v_l], inc=True)
            last_stg[sl] = v_wo

        _snap_phase(o, 3)
        vfb = bfv(C, 0, 128, TCH * 1024)
        qtb = fv(Dn, 0, 128, 32 * TCH)
        cqt = fv(Dn, 2048, 128, 16 * TCH)
        sqt = fv(Dn, 3072, 128, 16 * TCH)
        ckf = fv(Dn, 4096, 128, 16 * TCH)
        skf = fv(Dn, 5120, 128, 16 * TCH)
        rdeb = fv(Dn, 6144, 128, 16 * TCH)
        ckbb = bfv(Dn, 0, 128, 16 * TCH)
        skbb = bfv(Dn, 512, 128, 16 * TCH)
        ztc = [bfv(C, 57344, 128, 1024), bfv(C, 59392, 128, 1024)]
        hstg = [bfv(C, 32768, 128, 1024), bfv(C, 34816, 128, 1024)]
        tmp1 = fv(C, 61440, 128, 512)
        tmp2 = fv(C, 63488, 128, 512)
        mcb = bfv(A, 0, 128, TCH * 1024)
        msb = bfv(A, 32768, 128, TCH * 1024)
        hTb = bfv(B, 32768, 128, 8 * 2048)
        xnc = [fv(C, 36864, 128, 1024), fv(C, 40960, 128, 1024)]
        x1c = [fv(C, 45056, 128, 1024), fv(C, 49152, 128, 1024)]
        z2c = [bfv(C, 53248, 128, 1024), bfv(C, 55296, 128, 1024)]
        tdsb = fv(C, 36864, 16, 2048)
        tdd = fv(C, 45056, 16, 32)
        crd = [fv(C, 45184, 1, 32), fv(C, 45312, 1, 32)]
        crow = [fv(C, 0, 1, 2048), fv(C, 8192, 1, 2048)]
        v_mcms_pe_prev = 0
        v_proj_ev_prev = 0
        v_z2tr_prev = [0, 0]
        v_z2c_cons = [[], []]
        v_x1_cons = [[], []]
        v_hstg_cons = [[], []]
        for b in range(NB):
            last_ztc = [0, 0]
            v_qk_ev = 0
            v_tanh_prev = 0
            v_vf_ev = [0, 0]
            v_tanh = 0
            for sc in range(TCH):
                sl = sc % 2
                gc = b * TCH + sc
                v_z = o.dma_in(
                    ztc[sl][:].rearrange("p (c s) -> p c s", s=128),
                    hbm_blk(z3T_d)[:, :, b * T + sc * 128:b * T + sc * 128 + 128],
                    [("pe", last_ztc[sl])] if last_ztc[sl] else (),
                    ch=f"q{2 + sl}")
                pq = PS[4 + sl]
                w0 = [v_z, ("dve", max(v_wvf, v_rr, v_bvb, v_bqkb))]
                if b > 0 and sc == 0:
                    w0.append(("dve", v_proj_ev_prev))
                    w0 += [(g, o.v(g)) for g in GCH if o.v(g)]
                if v_vf_ev[sl]:
                    w0.append(("dve", v_vf_ev[sl]))
                wq = []
                if b > 0 and sc < 2:
                    wq.append(("act", v_z2tr_prev[sl]))
                if v_qk_ev:
                    wq.append(("dve", v_qk_ev))
                for dc in range(8):
                    lhsT = ztc[sl][:, dc * 128:(dc + 1) * 128]
                    o.mm(PS[2 * sl][:, :], lhsT,
                         wvb[:, dc * 1024:dc * 1024 + 512], dc == 0, dc == 7,
                         waits=w0 if dc == 0 else ())
                    o.mm(PS[2 * sl + 1][:, :], lhsT,
                         wvb[:, dc * 1024 + 512:dc * 1024 + 1024],
                         dc == 0, dc == 7)
                    v_mmq = o.mm(pq[:, 0:32], lhsT, wqkb[:, dc * 32:dc * 32 + 32],
                                 dc == 0, dc == 7,
                                 waits=wq if dc == 0 else (), inc=(dc == 7))
                last_ztc[sl] = v_mmq
                v_qkl = o.stt(qkl_t[:, :], pq[:, 0:32], rrbuf[:, gc:gc + 1],
                              bqkb[:, :], ALU.mult, ALU.add,
                              waits=[("pe", v_mmq), ("act", v_tanh_prev)],
                              inc=True)
                v_qk_ev = v_qkl
                if debug and b == 1 and sc == TCH - 1:
                    o.dma_out(dbgb[:, 32:64], qkl_t[:, :], [("dve", v_qkl)])
                o.actv(qtb[:, sc * 16:(sc + 1) * 16], qkl_t[:, 0:16],
                       ACT.Tanh, waits=[("dve", v_qkl)])
                v_tanh = o.actv(qtb[:, 256 + sc * 16:256 + (sc + 1) * 16],
                                qkl_t[:, 16:32], ACT.Tanh, inc=True)
                v_tanh_prev = v_tanh
                for half in range(2):
                    v_vf_ev[sl] = o.stt(
                        vfb[:, sc * 1024 + half * 512:sc * 1024 + half * 512 + 512],
                        PS[2 * sl + half][:, :], rrbuf[:, gc:gc + 1],
                        bvb[:, half * 512:half * 512 + 512], ALU.mult, ALU.add,
                        waits=[("pe", v_mmq)] if half == 0 else (),
                        inc=(half == 1))
            if b == 0:
                _snap_phase(o, 31)
            o.actv(cqt[:], qtb[:, 0:256], ACT.Sin, bias=HPI, scale=QPI,
                   waits=[("act", v_tanh)])
            o.actv(sqt[:], qtb[:, 0:256], ACT.Sin, bias=0.0, scale=QPI)
            o.actv(ckf[:], qtb[:, 256:512], ACT.Sin, bias=HPI, scale=QPI)
            v_sin = o.actv(skf[:], qtb[:, 256:512], ACT.Sin, bias=0.0,
                           scale=QPI, inc=True)
            v_ckbb = o.cp("dve", ckbb[:], ckf[:], [("act", v_sin)], True)
            v_skbb = o.cp("dve", skbb[:], skf[:], inc=True)

            for sc in range(TCH):
                o.mm(PS[6][0:16, 0:16], ejf[:, 15 - sc:31 - sc],
                     ckf[:, sc * 16:(sc + 1) * 16], sc == 0, sc == TCH - 1,
                     waits=[("act", v_sin)] if sc == 0 else ())
                v_td = o.mm(PS[6][0:16, 16:32], ejf[:, 15 - sc:31 - sc],
                            skf[:, sc * 16:(sc + 1) * 16], False,
                            sc == TCH - 1, inc=(sc == TCH - 1))
            v_tde = o.cp("dve", tdd[0:16, 0:32], PS[6][0:16, 0:32],
                         [("pe", v_td)], True)
            v_cd = o.mm(PS[7][0:1, 0:32], u16f[:, 1:2], tdd[0:16, 0:32],
                        True, True, waits=[("dve", v_tde)], inc=True)
            v_crd = [0, 0]
            v_crd[1] = o.cp("dve", crd[1][0:1, :], PS[7][0:1, 0:32],
                            [("pe", v_cd)], True)
            ksum = fv(C, 61440, 128, 512)
            v_ksum = 0
            for sc in range(TCH):
                pb = PS[0 if sc % 2 == 0 else 2]
                wB3 = [("dve", v_tde)] if sc < 2 else [("dve", v_ksum)]
                if sc >= 1:
                    wB3 = wB3 + [("dve", v_crd[sc % 2])]
                if sc == 0:
                    o.mm(pb[:, 0:16], trif[:, :], ckf[:, 0:16],
                         True, True, waits=wB3)
                    v_mm = o.mm(pb[:, 16:32], trif[:, :], skf[:, 0:16],
                                False, True, inc=True)
                else:
                    o.mm(pb[:, 0:16], trif[:, :], ckf[:, sc * 16:(sc + 1) * 16],
                         True, False, waits=wB3)
                    o.mm(pb[:, 0:16], onesrf[:, :], crd[sc % 2][0:1, 0:16],
                         False, True)
                    o.mm(pb[:, 16:32], trif[:, :], skf[:, sc * 16:(sc + 1) * 16],
                         False, False)
                    v_mm = o.mm(pb[:, 16:32], onesrf[:, :],
                                crd[sc % 2][0:1, 16:32], False, True, inc=True)
                if sc + 2 < TCH:
                    v_cd = o.mm(PS[7][0:1, 0:32], u16f[:, sc + 2:sc + 3],
                                tdd[0:16, 0:32], True, True,
                                waits=[("dve", max(v_crd))], inc=True)
                    v_crd[sc % 2] = o.cp("dve", crd[sc % 2][0:1, :],
                                         PS[7][0:1, 0:32],
                                         [("pe", v_cd)], True)
                v_ksum = o.cp("dve", ksum[:, sc * 32:(sc + 1) * 32],
                              pb[:, 0:32], [("pe", v_mm)], True)
            if debug and b == 0:
                o.dma_out(dbgb[:, 64:96], ksum[:, 32:64],
                          [("dve", o.v("dve"))])
                o.dma_out(cr_dbg[32:33, 0:32], tdd[0:16, :].rearrange(
                    "p n -> (p n)").unsqueeze(0) if False else tdd[0:1, 0:32], [])
                o.dma_out(cr_dbg[33:34, 0:32], crd[1][0:1, :], [])
            kview = ksum[:].rearrange("p (c k) -> p c k", k=32)
            cqv_ = cqt[:].rearrange("p (c k) -> p c k", k=16)
            sqv_ = sqt[:].rearrange("p (c k) -> p c k", k=16)
            rdv_ = rdeb[:].rearrange("p (c k) -> p c k", k=16)
            ckscr = ckf[:].rearrange("p (c k) -> p c k", k=16)
            v_d1 = o.tt(rdv_, kview[:, :, 0:16], cqv_, ALU.mult,
                        waits=[("dve", v_ksum)], inc=True)
            v_d2 = o.tt(ckscr, kview[:, :, 16:32], sqv_, ALU.mult,
                        inc=True)
            v_d3 = o.tt(rdv_, rdv_, ckscr, ALU.add,
                        waits=[("dve", v_d2)], inc=True)
            v_d4 = o.ts(rdeb[:], rdeb[:], 1e-6, None, ALU.add,
                        waits=[("dve", v_d3)], inc=True)
            v_rd = o.recip(rdeb[:], rdeb[:], waits=[("dve", v_d4)], inc=True)
            if b == 0:
                _snap_phase(o, 32)
            if debug:
                qo = 1024 * b
                o.dma_out(dbgq[:, qo + 0:qo + 256], cqt[:],
                          [("dve", o.v("dve")), ("act", o.v("act"))])
                o.dma_out(dbgq[:, qo + 256:qo + 512], sqt[:], [])
                o.dma_out(dbgq[:, qo + 512:qo + 768], rdeb[:], [])
                o.dma_out(dbgq[:, qo + 768:qo + 1024], ckf[:], [])
            v_mcms = 0
            v_t = 0
            for sc in range(TCH):
                wC = [] if sc else ([("pe", v_mcms_pe_prev)] if v_mcms_pe_prev else [])
                ck0 = ckbb[:, sc * 16:(sc + 1) * 16].unsqueeze(2).broadcast_to(
                    [128, 16, 64])
                sk0 = skbb[:, sc * 16:(sc + 1) * 16].unsqueeze(2).broadcast_to(
                    [128, 16, 64])
                vfv = vfb[:, sc * 1024:(sc + 1) * 1024].rearrange(
                    "p (i h) -> p i h", h=64)
                tmpv = fv(C, 61440, 128, 1024)
                tmpv3 = tmpv.rearrange("p (i h) -> p i h", h=64)
                if sc == 0:
                    wC = wC + [("dve", v_skbb)]
                o.tt(tmpv3, vfv, ck0, ALU.mult, waits=wC)
                o.cp("dve", mcb[:, sc * 1024:(sc + 1) * 1024], tmpv)
                o.tt(tmpv3, vfv, sk0, ALU.mult)
                v_mcms = o.cp("dve", msb[:, sc * 1024:(sc + 1) * 1024], tmpv,
                              inc=True)
                for q, ps0 in ((0, 4), (1, 6)):
                    srcq = mcb if q == 0 else msb
                    for half in range(2):
                        v_t = o.mm(PS[ps0 + half][0:16, :],
                                   ejb[:, 15 - sc:31 - sc],
                                   srcq[:, sc * 1024 + half * 512:sc * 1024 + half * 512 + 512],
                                   sc == 0, sc == TCH - 1,
                                   waits=[("dve", v_mcms)] if (q == 0 and half == 0) else (),
                                   inc=(sc == TCH - 1 and q == 1 and half == 1))
            if b == 0:
                _snap_phase(o, 39)
            if b == 0 and debug:
                o.dma_out(dbgm[:, 0:1024], mcb[:, 0:1024],
                          [("dve", o.v("dve"))])
                o.dma_out(dbgm[:, 1024:2048], mcb[:, 1024:2048], [])
            v_te = 0
            for i, ps in enumerate((4, 5, 6, 7)):
                v_te = o.cp("act", tdsb[0:16, i * 512:(i + 1) * 512],
                            PS[ps][0:16, :], [("pe", v_t)], True)
            v_te = ("act", v_te)
            if b == 0:
                _snap_phase(o, 40)
            if b == 0:
                _snap_phase(o, 335)
            v_cw = 0
            for i in range(4):
                v_cw = o.mm(PS[i][0:1, :], u16f[:, 1:2],
                            tdsb[0:16, i * 512:(i + 1) * 512], True, True,
                            waits=[v_te] if i == 0 else (),
                            inc=(i == 3))
            v_crow = [0, 0]
            for i in range(4):
                v_crow[1] = o.cp("dve", crow[1][0:1, i * 512:(i + 1) * 512],
                                 PS[i][0:1, :],
                                 [("pe", v_cw)] if i == 0 else (), inc=(i == 3))
            if b == 0 and debug:
                o.dma_out(cr_dbg[0:16, :], tdsb[0:16, :],
                          [("dve", o.v("dve"))])
                o.dma_out(cr_dbg[16:17, :], crow[1][0:1, :],
                          [("dve", o.v("dve"))])
            if b == 0:
                _snap_phase(o, 33)
            v_heads_tt = {}
            v_tr_done = {}
            v_hT_ev = 0
            for sc in range(TCH):
                if b == 0 and sc <= 3:
                    _snap_phase(o, 330 + sc)
                sl = sc % 2
                p4 = 4 * sl
                wD = [("dve", v_crow[sl] if sc >= 1 else v_crow[1])]
                if sc >= 2:
                    wD += [("act", v_tr_done[sc - 2]),
                           ("dve", v_heads_tt[sc - 2])]
                for q in range(2):
                    srcb = mcb if q == 0 else msb
                    for half in range(2):
                        pb = PS[p4 + 2 * q + half]
                        if sc == 0:
                            v_mmD = o.mm(pb[:, :], trib[:, :],
                                         srcb[:, sc * 1024 + half * 512:sc * 1024 + half * 512 + 512],
                                         True, True,
                                         waits=wD if (q == 0 and half == 0) else (),
                                         inc=(q == 1 and half == 1))
                        else:
                            o.mm(pb[:, :], trib[:, :],
                                 srcb[:, sc * 1024 + half * 512:sc * 1024 + half * 512 + 512],
                                 True, False,
                                 waits=wD if (q == 0 and half == 0) else ())
                            v_mmD = o.mm(pb[:, :], onesrf[:, :],
                                         crow[sl][0:1, q * 1024 + half * 512:q * 1024 + half * 512 + 512],
                                         False, True, inc=(q == 1 and half == 1))
                for half in range(2):
                    cq0 = cqt[:, sc * 16 + half * 8:sc * 16 + half * 8 + 8] \
                        .unsqueeze(2).broadcast_to([128, 8, 64])
                    sq0 = sqt[:, sc * 16 + half * 8:sc * 16 + half * 8 + 8] \
                        .unsqueeze(2).broadcast_to([128, 8, 64])
                    rd0 = rdeb[:, sc * 16 + half * 8:sc * 16 + half * 8 + 8] \
                        .unsqueeze(2).broadcast_to([128, 8, 64])
                    t1v = tmp1[:].rearrange("p (i h) -> p i h", h=64)
                    t2v = tmp2[:].rearrange("p (i h) -> p i h", h=64)
                    wh = [("pe", v_mmD)]
                    if half == 0:
                        wh += v_hstg_cons[sl]
                    o.tt(t1v, PS[p4 + half][:, :].rearrange(
                        "p (i h) -> p i h", h=64), cq0, ALU.mult,
                        waits=wh if half == 0 else [("pe", v_mmD)])
                    o.tt(t2v, PS[p4 + 2 + half][:, :].rearrange(
                        "p (i h) -> p i h", h=64), sq0, ALU.mult)
                    o.tt(tmp1[:, :], tmp1[:, :], tmp2[:, :], ALU.add)
                    o.tt(t2v, t1v, rd0, ALU.mult)
                    v_h = o.cp("dve",
                               hstg[sl][:, half * 512:half * 512 + 512],
                               tmp2[:, :], inc=(half == 1))
                v_heads_tt[sc] = v_h
                if debug and b == 0 and sc in (0, 1):
                    o.dma_out(dbgh[:, (0 if sc == 0 else 1024):(1024 if sc == 0 else 2048)],
                              hstg[sl][:, :], [("dve", v_h)])
                ptr = PSB[p4]
                for ncv in range(8):
                    v_trh = o.tr(ptr[:, ncv * 128:(ncv + 1) * 128],
                                 hstg[sl][:, ncv * 128:(ncv + 1) * 128],
                                 identb[:],
                                 waits=[("dve", v_h)] if ncv == 0 else (),
                                 inc=(ncv == 7))
                v_hstg_cons[sl] = [("pe", v_trh)]
                dstv = hTb[:].rearrange("p (c s) -> p c s", c=8)[:, :, sc * 128:(sc + 1) * 128]
                srcv = ptr[:].rearrange("p (c s) -> p c s", c=8)
                v_hT_ev = o.cp("act", dstv, srcv, [("pe", v_trh)], True)
                v_tr_done[sc] = v_hT_ev
                if sc + 2 < TCH:
                    v_cw = 0
                    for i in range(4):
                        v_cw = o.mm(PS[p4 + i][0:1, :], u16f[:, sc + 2:sc + 3],
                                    tdsb[0:16, i * 512:(i + 1) * 512], True,
                                    True,
                                    waits=[("dve", v_h), ("act", v_hT_ev)]
                                    if i == 0 else (),
                                    inc=(i == 3))
                    for i in range(4):
                        v_crow[sl] = o.cp(
                            "dve", crow[sl][0:1, i * 512:(i + 1) * 512],
                            PS[p4 + i][0:1, :],
                            [("pe", v_cw)] if i == 0 else (), inc=(i == 3))
                    if b == 0 and debug:
                        o.dma_out(cr_dbg[17 + sc:18 + sc, :], crow[sl][0:1, :],
                                  [("dve", v_crow[sl])])
            if debug:
                o.dma_out(hbm_blk(hT_dbg[b]), hTb[:].rearrange(
                    "p (c s) -> p c s", c=8), [("act", v_hT_ev)])
            if b == 0:
                _snap_phase(o, 34)
            last_xn = [0, 0]
            v_projmm = 0
            v_proj_ev = [0, 0]
            for sc in range(TCH):
                sl = sc % 2
                gc = b * TCH + sc
                v_xl = o.dma_in(xnc[sl][:, :],
                                xn_d[gc * 128:(gc + 1) * 128, :],
                                ([("dve", last_xn[sl])] if last_xn[sl] else []) +
                                ([("pe", o.v("pe"))] if sc < 2 else []),
                                ch=f"q{4 + sl}")
                w0 = [("act", v_hT_ev), ("dve", v_wo)]
                if sc >= 2:
                    w0 = [("dve", v_proj_ev[sl])]
                for ncv in range(8):
                    lhsT = hTb[:, ncv * 2048 + sc * 128:ncv * 2048 + sc * 128 + 128]
                    o.mm(PS[2 * sl][:, :], lhsT,
                         wob[:, ncv * 1024:ncv * 1024 + 512], ncv == 0,
                         ncv == 7, waits=w0 if ncv == 0 else ())
                    v_projmm = o.mm(PS[2 * sl + 1][:, :], lhsT,
                                    wob[:, ncv * 1024 + 512:ncv * 1024 + 1024],
                                    ncv == 0, ncv == 7, inc=(ncv == 7))
                for half in range(2):
                    v_x1tt = o.tt(x1c[sl][:, half * 512:half * 512 + 512],
                                  PS[2 * sl + half][:, :],
                                  xnc[sl][:, half * 512:half * 512 + 512],
                                  ALU.add,
                                  waits=([("pe", v_projmm), v_xl] +
                                         v_x1_cons[sl]) if half == 0 else (),
                                  inc=(half == 1))
                last_xn[sl] = v_x1tt
                o.op("dve", lambda e, s=sl: e.bn_stats(
                    out=st2[:, 0:6], in_=x1c[s][:, 0:512]))
                v_bs2 = o.op("dve", lambda e, s=sl: e.bn_stats(
                    out=st2[:, 6:12], in_=x1c[s][:, 512:1024]), inc="dve")
                v_ag2 = o.op("dve", lambda e, c=gc: e.bn_aggr(
                    out=stat[:, 2 * c:2 * c + 2], in_=st2[:, :]),
                    [("dve", v_bs2)], inc="dve")
                mu2 = stat[:, 2 * gc:2 * gc + 1]
                var2 = stat[:, 2 * gc + 1:2 * gc + 2]
                v1 = o.ts(smt[:, 0:1], var2, 1e-5, None, ALU.add,
                          waits=[("dve", v_ag2)], inc=True)
                v2 = o.actv(smt[:, 1:2], smt[:, 0:1], ACT.Sqrt,
                            waits=[("dve", v1)], inc=True)
                v_rcp = o.recip(r2buf[:, gc:gc + 1], smt[:, 1:2],
                                waits=[("act", v2)], inc=True)
                v_br = o.stt(b2rb[:, gc:gc + 1], mu2, -1.0, r2buf[:, gc:gc + 1],
                             ALU.mult, ALU.mult, waits=[("dve", v_rcp)],
                             inc=True)
                v_z2 = o.ts(z2c[sl][:, :], x1c[sl][:, :], r2buf[:, gc:gc + 1],
                            b2rb[:, gc:gc + 1], ALU.mult, ALU.add,
                            waits=v_z2c_cons[sl] + [("dve", v_br)], inc=True)
                v_proj_ev[sl] = v_z2
                v_x1d = o.dma_out(x1_d[gc * 128:(gc + 1) * 128, :],
                                  x1c[sl][:, :], [("dve", v_z2)],
                                  ch=f"g{sl}")
                v_x1_cons[sl] = [v_x1d]
                ptz = PSB[4 + sl]
                wtz = [("dve", v_z2)]
                if b > 0 or sc >= 2:
                    wtz.append(("act", v_z2tr_prev[sl]))
                for dc in range(8):
                    v_trz = o.tr(ptz[:, dc * 128:(dc + 1) * 128],
                                 z2c[sl][:, dc * 128:(dc + 1) * 128],
                                 identb[:], waits=wtz if dc == 0 else (),
                                 inc=(dc == 7))
                v_z2c_cons[sl] = [("pe", v_trz)]
                wze = [("pe", v_trz)] + v_hstg_cons[sl]
                v_ze = o.cp("act", hstg[sl][:, :], ptz[:, :], wze, True)
                v_z2tr_prev[sl] = v_ze
                v_hstg_cons[sl] = [("act", v_ze)]
                v_z2td = o.dma_out(hbm_blk(z2T_d)[:, :, gc * 128:(gc + 1) * 128],
                                   hstg[sl][:].rearrange("p (c s) -> p c s", c=8),
                                   [("act", v_ze)], ch=f"g{2 + sl}")
                v_hstg_cons[sl].append(v_z2td)
            v_mcms_pe_prev = o.v("pe")
            v_proj_ev_prev = max(v_proj_ev)

        _snap_phase(o, 4)
        bw = o.barrier_waits()
        wstg = [fv(C, 0, 128, 4096), fv(C, 16384, 128, 4096)]
        b1row = fv(C, 32768, 1, 4096)
        v_b1l = o.dma_in(b1row[0:1, :], b1a_d[:, :], bw, ch="q6")
        w1ab = bfv(A, 0, 128, 8 * HID)
        w2ab = bfv(B, 0, 128, 32 * 1024)
        last_stg = [0, 0]
        v_w1cast = 0
        for dc in range(8):
            sl = dc % 2
            v_l = o.dma_in(wstg[sl][:, :], w1a_d[dc * 128:(dc + 1) * 128, :],
                           ([("dve", last_stg[sl])] if last_stg[sl] else []) +
                           (bw if dc < 2 else []), ch=f"q{sl}")
            v_c = o.cp("dve", w1ab[:, dc * HID:(dc + 1) * HID], wstg[sl][:, :],
                       waits=[v_l], inc=True)
            last_stg[sl] = v_c
            v_w1cast = v_c
        _snap_phase(o, 51)
        v_b1p = 0
        for j in range(8):
            for dc in range(8):
                wj = ()
                if dc == 0:
                    wj = [("dve", v_w1cast if j == 0 else v_b1p)]
                v_mm = o.mm(PS[0][0:1, :], vecsb[:, LN2B + dc:LN2B + dc + 1],
                            w1ab[:, dc * HID + j * 512:dc * HID + j * 512 + 512],
                            dc == 0, dc == 7,
                            waits=wj, inc=(dc == 7))
            v_b1p = o.tt(b1row[0:1, j * 512:(j + 1) * 512],
                         b1row[0:1, j * 512:(j + 1) * 512], PS[0][0:1, :],
                         ALU.add, waits=[("pe", v_mm), v_b1l],
                         inc=True)
        _snap_phase(o, 52)
        for hc in range(32):
            v_bt = o.mm(PS[1][:, hc:hc + 1],
                        b1row[0:1, hc * 128:(hc + 1) * 128],
                        onesrf[0:1, 0:1], hc == 0, hc == 31,
                        waits=[("dve", v_b1p)] if hc == 0 else (),
                        inc=(hc == 31))
        v_b1pp = o.cp("dve", b1pp[:, :], PS[1][:, 0:32], [("pe", v_bt)], True)
        _snap_phase(o, 53)
        v_w1f = 0
        for dc in range(8):
            v_w1f = o.ts(w1ab[:, dc * HID:(dc + 1) * HID],
                         w1ab[:, dc * HID:(dc + 1) * HID], ln2w(dc), None,
                         ALU.mult, waits=[("pe", v_bt)] if dc == 0 else (),
                         inc=(dc == 7))
        v_w2 = 0
        for hc in range(32):
            sl = hc % 2
            v_l = o.dma_in(fv(C, sl * 16384, 128, 1024)[:, :],
                           w2a_d[hc * 128:(hc + 1) * 128, :],
                           [("dve", last_stg[sl])] if last_stg[sl] else (),
                           ch=f"q{2 + sl}")
            v_w2 = o.cp("dve", w2ab[:, hc * 1024:(hc + 1) * 1024],
                        fv(C, sl * 16384, 128, 1024)[:, :],
                        waits=[v_l], inc=True)
            last_stg[sl] = v_w2
        b2b = fv(C, 49152, 128, 1024)
        b2row = fv(C, 53248, 1, 1024)
        rowbB = bfv(C, 57344, 1, 512)
        v_b2l = o.dma_in(b2row[0:1, :], b2r_d[:, :], ch="q6")
        v_b2b = 0
        for half in range(2):
            v_rb = o.cp("dve", rowbB[0:1, :], b2row[0:1, half * 512:half * 512 + 512],
                        waits=[v_b2l], inc=True)
            v_bc = o.mm(PS[2][:, :], onesrb[:, :], rowbB[0:1, :], True, True,
                        waits=[("dve", v_rb)], inc=True)
            v_b2b = o.cp("dve", b2b[:, half * 512:half * 512 + 512], PS[2][:, :],
                         [("pe", v_bc)], True)
        if debug:
            o.dma_out(dbgw[:, 0:32768], w1ab[:, :], [("dve", v_w1f)])
            o.dma_out(dbgw[:, 32768:65536], w2ab[:, :], [("dve", v_w2)])
            o.dma_out(dbgq[:, 0:32], b1pp[:, :], [("dve", v_b1pp)])
            o.dma_out(dbgq[:, 1024:2048], b2b[:, :], [("dve", v_b2b)])
        _snap_phase(o, 5)
        bw = o.barrier_waits()
        z2tg = [bfv(C, 0, 128, 2048), bfv(C, 4096, 128, 2048)]
        h1T = bfv(C, 8192, 128, 32 * 256)
        x1g = [fv(C, 24576, 128, 1024), fv(C, 28672, 128, 1024)]
        xg = [fv(C, 32768, 128, 1024), fv(C, 36864, 128, 1024)]
        og = [fv(C, 40960, 128, 1024), fv(C, 45056, 128, 1024)]
        v_h1_cons = 0
        v_og_dma = [0, 0]
        v_x1g_cons = [0, 0]
        v_ps_rd = [0, 0]
        for g in range(16):
            sl = g % 2
            v_zl = o.dma_in(
                z2tg[sl][:].rearrange("p (c s) -> p c s", s=256),
                hbm_blk(z2T_d)[:, :, g * 256:(g + 1) * 256],
                ([("pe", v_h1_cons)] if v_h1_cons else []) + (bw if g < 2 else []),
                ch=f"q{sl}")
            w_h1 = [v_zl, ("dve", max(v_w1f, v_w2, v_b1pp))]
            v_gel = 0
            gels = {}
            for hc in range(32):
                pb = PS[hc % 4]
                wx = list(w_h1) if hc == 0 else []
                if hc >= 4:
                    wx.append(("act", gels[hc - 4]))
                for dc in range(8):
                    v_hmm = o.mm(pb[:, 0:256],
                                 w1ab[:, dc * HID + hc * 128:dc * HID + hc * 128 + 128],
                                 z2tg[sl][:, dc * 256:(dc + 1) * 256],
                                 dc == 0, dc == 7, waits=wx if dc == 0 else (),
                                 inc=(dc == 7))
                v_gel = o.actv(h1T[:, hc * 256:(hc + 1) * 256], pb[:, 0:256],
                               ACT.Gelu_apprx_tanh, bias=b1pp[:, hc:hc + 1],
                               waits=[("pe", v_hmm)], inc=True)
                gels[hc] = v_gel
            if debug and g == 3:
                o.dma_out(dbgh[:, :], h1T[:, 0:2048], [("act", v_gel)])
                o.dma_out(dbgm[:, :], z2tg[sl][:, 0:2048], [("act", v_gel)])
            for si in range(2):
                sc = 2 * g + si
                v_x1l = o.dma_in(x1g[sl][:, :], x1_d[sc * 128:(sc + 1) * 128, :],
                                 [("dve", v_x1g_cons[sl])] if v_x1g_cons[sl] else (),
                                 ch=f"q{2 + sl}")
                v_xl = o.dma_in(xg[sl][:, :], x_d[sc * 128:(sc + 1) * 128, :],
                                [("dve", v_x1g_cons[sl])] if v_x1g_cons[sl] else (),
                                ch=f"q{4 + sl}")
                for hc in range(32):
                    lhsT = h1T[:, hc * 256 + si * 128:hc * 256 + si * 128 + 128]
                    o.mm(PS[4 + 2 * si][:, :], lhsT,
                         w2ab[:, hc * 1024:hc * 1024 + 512], hc == 0, hc == 31,
                         waits=([("act", v_gel)] +
                                ([("dve", v_ps_rd[si])] if v_ps_rd[si] else []))
                         if hc == 0 else ())
                    v_mmm = o.mm(PS[5 + 2 * si][:, :], lhsT,
                                 w2ab[:, hc * 1024 + 512:hc * 1024 + 1024],
                                 hc == 0, hc == 31, inc=(hc == 31))
                for half in range(2):
                    o.tt(og[sl][:, half * 512:half * 512 + 512],
                         PS[4 + 2 * si + half][:, :],
                         b2b[:, half * 512:half * 512 + 512], ALU.add,
                         waits=([("pe", v_mmm), v_xl, ("dve", v_b2b)] +
                                ([v_og_dma[sl]] if v_og_dma[sl] else []))
                         if half == 0 else ())
                o.tt(og[sl][:, :], og[sl][:, :], x1g[sl][:, :], ALU.add,
                     waits=[v_x1l])
                v_og = o.tt(og[sl][:, :], og[sl][:, :], xg[sl][:, :], ALU.add,
                            inc=True)
                v_x1g_cons[sl] = v_og
                v_ps_rd[si] = v_og
                v_og_dma[sl] = o.dma_out(outp_d[sc * 128:(sc + 1) * 128, :],
                                         og[sl][:, :], [("dve", v_og)],
                                         ch=f"g{4 + sl}")
            v_h1_cons = o.v("pe")

        _snap_phase(o, 6)
        bw = o.barrier_waits()
        seTb = bfv(A, 0, 128, 8 * NSL)
        z3Tb = bfv(B, 0, 128, 8 * T)
        z3b = bfv(B, 32768, 128, TCH * 1024)
        Eb = [bfv(C, 0, 128, 2048), bfv(C, 4096, 128, 2048)]
        t4st = [bfv(C, 8192, 128, 2048), bfv(C, 12288, 128, 2048)]
        dispg = bfv(C, 16384, 128, 16 * 512)
        slst = [bfv(C, 32768, 128, 512), bfv(C, 33792, 128, 512)]
        lTsb = [fv(C, 36864, 128, 2048), fv(C, 45056, 128, 2048)]
        v_seT = 0
        for dc in range(8):
            v_seT = o.dma_in(
                seTb[:].rearrange("p (c n) -> p c n", c=8)[:, dc:dc + 1, :].squeeze(1),
                hbm_blk(seT_d)[:, dc:dc + 1, :].squeeze(1),
                bw if dc < 2 else (), ch="q6")
        v_logits_pe_b0 = 0
        for b in range(NB):
            wzb = [("pe", v_logits_pe_b0)] if b else []
            v_zl = 0
            for dc in range(8):
                v_zl = o.dma_in(
                    z3Tb[:].rearrange("p (c s) -> p c s", c=8)[:, dc:dc + 1, :].squeeze(1),
                    hbm_blk(z3T_d)[:, dc:dc + 1, b * T:(b + 1) * T].squeeze(1),
                    wzb if dc == 0 else (), ch="q0")
            v_z3bl = o.dma_in(z3b[:].rearrange("p (c s) -> p c s", c=TCH),
                              hbm_blk(z3_d)[:, b * TCH:(b + 1) * TCH, :], wzb,
                              ch="q1")
            v_ppm = 0
            for j in range(2):
                for sg in range(4):
                    for dc in range(8):
                        lhsT = seTb[:, dc * NSL + j * 2048:dc * NSL + (j + 1) * 2048] \
                            .rearrange("p (n k) -> p k n", k=16)[:, 0:1, :].squeeze(1)
                        v_mmp = o.mm(PS[0][:, :], lhsT,
                                     z3Tb[:, dc * T + sg * 512:dc * T + sg * 512 + 512],
                                     dc == 0, dc == 7,
                                     waits=([v_seT, v_zl] +
                                            ([("dve", v_ppm)] if v_ppm else []))
                                     if dc == 0 else (),
                                     inc=(dc == 7))
                    first = (j == 0 and sg == 0)
                    v_ppm = o.red(smt[:, 0:1], PS[0][:, :], ALU.max,
                                  waits=[("pe", v_mmp)], inc=True)
                    if first:
                        o.cp("dve", smt[:, 1:2], smt[:, 0:1],
                             waits=[("dve", v_ppm)])
                    else:
                        v_ppm2 = o.tt(smt[:, 1:2], smt[:, 1:2], smt[:, 0:1],
                                      ALU.max, waits=[("dve", v_ppm)],
                                      inc=True)
            mbv = qkl_t[:, 4:5].bitcast(BF16)[:, 0:1]
            v_ppc = o.cp("dve", mbv, smt[:, 1:2],
                         waits=[("dve", v_ppm2)], inc=True)
            v_trk = o.tr(PSB[1][0:1, 0:128], mbv, identb[:],
                         waits=[("dve", v_ppc)], inc=True)
            v_k0 = o.red(smt[0:1, 15:16], PSB[1][0:1, 0:128], ALU.max,
                         waits=[("pe", v_trk)], inc=True)
            v_k0a = o.ts(k0r[0:1, 2 * b:2 * b + 1], smt[0:1, 15:16], 50.0,
                         None, ALU.add, waits=[("dve", v_k0)], inc=True)
            v_k0r = o.ts(k0r[0:1, 2 * b + 1:2 * b + 2],
                         k0r[0:1, 2 * b:2 * b + 1], 0.25, None, ALU.mult,
                         waits=[("dve", v_k0a)], inc=True)
            v_k0b = o.mm(PS[1][:, 0:1], onesrf[:, :],
                         k0r[0:1, 2 * b + 1:2 * b + 2], True, True,
                         waits=[("dve", v_k0r)], inc=True)
            v_mk0 = o.ts(mk0q[:, b:b + 1], PS[1][:, 0:1], -1.0, None, ALU.mult,
                         waits=[("pe", v_k0b)], inc=True)
            v_lT_free = 0
            v_t4dma = [0, 0]
            v_E_cons = [[], []]
            v_slots_pe = 0
            v_dispg_cons = []
            v_trd_prev = 0
            for ncv in range(32):
                pl = ncv % 2
                w0 = [v_seT, v_zl, ("dve", v_mk0)]
                if v_lT_free:
                    w0 = [("act", v_lT_free)]
                v_lmm = 0
                for q in range(4):
                    for dc in range(8):
                        v_lmm = o.mm(PS[q][:, :],
                                     seTb[:, dc * NSL + ncv * 128:dc * NSL + ncv * 128 + 128],
                                     z3Tb[:, dc * T + q * 512:dc * T + q * 512 + 512],
                                     dc == 0, dc == 7,
                                     waits=w0 if (q == 0 and dc == 0) else (),
                                     inc=(q == 3 and dc == 7))
                v_r4 = 0
                for q in range(4):
                    v_r4 = o.red(smt[:, q:q + 1], PS[q][:, :], ALU.max,
                                 waits=[("pe", v_lmm)] if q == 0 else (),
                                 inc=(q == 3))
                v_cm = o.op("dve", lambda e: e.tensor_reduce(
                    out=smt[:, 6:7], in_=smt[:, 0:4], axis=AX, op=ALU.max,
                    negate=True), waits=[("dve", v_r4)], inc="dve")
                v_lc = 0
                for q in range(4):
                    v_lc = o.cp("act", lTsb[pl][:, q * 512:(q + 1) * 512],
                                PS[q][:, :],
                                waits=[("dve", v_cm)] if q == 0 else (),
                                inc=(q == 3))
                v_lT_free = v_lc
                v_E = o.actv(Eb[pl][:, :], lTsb[pl][:, :], ACT.Exp,
                             bias=smt[:, 6:7],
                             waits=v_E_cons[pl], inc=True, accum=smt[:, 7:8])
                v_ics = o.recip(smt[:, 8:9], smt[:, 7:8],
                                waits=[("act", v_E)], inc=True)
                v_dsp = o.ts(Eb[pl][:, :], Eb[pl][:, :], smt[:, 8:9], None,
                             ALU.mult, waits=[("dve", v_ics)], inc=True)
                v_t4 = o.actv(t4st[pl][:, :], lTsb[pl][:, :], ACT.Exp,
                              bias=mk0q[:, b:b + 1], scale=0.25,
                              waits=[v_t4dma[pl]] if v_t4dma[pl] else (),
                              inc=True)
                v_t4dma[pl] = o.dma_out(hbm_blk(t4_d[b])[:, ncv:ncv + 1, :].squeeze(1),
                                        t4st[pl][:, :], [("act", v_t4)],
                                        ch=f"g{6 + pl}")
                wtr = [("dve", v_dsp)]
                if v_trd_prev:
                    wtr.append(("dve", v_trd_prev))
                for scv in range(16):
                    pbank = PSB[4 + scv // 8]
                    v_trd = o.tr(pbank[:, (scv % 8) * 128:(scv % 8) * 128 + 128],
                                 Eb[pl][:, scv * 128:(scv + 1) * 128],
                                 identb[:], waits=wtr if scv == 0 else (),
                                 inc=(scv == 15))
                v_E_cons[pl] = [("pe", v_trd)]
                j = ncv % 4
                wev = [("pe", v_trd)] + v_dispg_cons
                v_ev1 = o.cp("dve",
                             dispg[:].rearrange("p (c n) -> p c n", c=16)[:, 0:8, j * 128:(j + 1) * 128],
                             PSB[4][:].rearrange("p (c n) -> p c n", c=8),
                             wev, True)
                v_psT_ev = o.cp("dve",
                                dispg[:].rearrange("p (c n) -> p c n", c=16)[:, 8:16, j * 128:(j + 1) * 128],
                                PSB[5][:].rearrange("p (c n) -> p c n", c=8),
                                inc=True)
                v_trd_prev = v_psT_ev
                if ncv % 4 == 3:
                    gslot = ncv // 4
                    v_sl_ev = [0, 0]
                    v_sl_dma = [0, 0]
                    for dc in range(8):
                        pb = PS[6 + dc % 2]
                        wsl = [("dve", v_psT_ev), v_z3bl]
                        if v_sl_ev[dc % 2]:
                            wsl.append(("dve", v_sl_ev[dc % 2]))
                        for scv in range(16):
                            v_smm = o.mm(pb[:, :],
                                         z3b[:, scv * 1024 + dc * 128:scv * 1024 + dc * 128 + 128],
                                         dispg[:, scv * 512:(scv + 1) * 512],
                                         scv == 0, scv == 15,
                                         waits=wsl if scv == 0 else (),
                                         inc=(scv == 15))
                        v_se2 = o.ts(slst[dc % 2][:, :], pb[:, :], 1.0, None,
                                     ALU.mult,
                                     waits=[("pe", v_smm)] +
                                     ([v_sl_dma[dc % 2]] if v_sl_dma[dc % 2] else []),
                                     inc=True)
                        v_sl_ev[dc % 2] = v_se2
                        v_sl_dma[dc % 2] = o.dma_out(
                            slT_d[b][dc * 128:(dc + 1) * 128,
                                     gslot * 512:(gslot + 1) * 512],
                            slst[dc % 2][:, :], [("dve", v_se2)],
                            ch=f"g{dc % 2}")
                    v_slots_pe = o.v("pe")
                    v_dispg_cons = [("pe", v_slots_pe)]
            if b == 0:
                v_logits_pe_b0 = o.v("pe")

        _snap_phase(o, 7)
        bw = o.barrier_waits()
        w1fb = bfv(A, 0, 128, 8 * HID)
        w2fb = bfv(B, 0, 128, 32 * 1024)
        last_stg = [0, 0]
        v_w1ff = 0
        for dc in range(8):
            sl = dc % 2
            v_l = o.dma_in(fv(C, sl * 16384, 128, 4096)[:, :],
                           w1f_d[dc * 128:(dc + 1) * 128, :],
                           ([("dve", last_stg[sl])] if last_stg[sl] else []) +
                           (bw if dc < 2 else []), ch=f"q{2 + sl}")
            v_w1ff = o.ts(w1fb[:, dc * HID:(dc + 1) * HID],
                          fv(C, sl * 16384, 128, 4096)[:, :], gxw(dc), None,
                          ALU.mult, waits=[v_l], inc=True)
            last_stg[sl] = v_w1ff
        v_w2f = 0
        for hc in range(32):
            sl = hc % 2
            v_l = o.dma_in(fv(C, sl * 16384, 128, 1024)[:, :],
                           w2f_d[hc * 128:(hc + 1) * 128, :],
                           [("dve", last_stg[sl])] if last_stg[sl] else (),
                           ch=f"q{4 + sl}")
            v_w2f = o.cp("dve", w2fb[:, hc * 1024:(hc + 1) * 1024],
                         fv(C, sl * 16384, 128, 1024)[:, :],
                         waits=[v_l], inc=True)
            last_stg[sl] = v_w2f
        slg = [bfv(C, 0, 128, 4096), bfv(C, 8192, 128, 4096)]
        hTg = bfv(C, 16384, 128, 32 * 512)
        pstb = [bfv(C, 49152, 128, 1024), bfv(C, 51200, 128, 1024)]
        v_hT_pe = 0
        v_smlp_pe_b0 = 0
        for b in range(NB):
            v_pst_dma = [0, 0]
            for g in range(8):
                sl = g % 2
                v_sl = o.dma_in(
                    slg[sl][:].rearrange("p (c n) -> p c n", c=8),
                    hbm_blk(slT_d[b])[:, :, g * 512:(g + 1) * 512],
                    ([("pe", v_hT_pe)] if v_hT_pe else []) +
                    (bw if (b == 0 and g == 0) else []), ch=f"q{sl}")
                v_gel = 0
                gels = {}
                for hc in range(32):
                    pb = PS[hc % 4]
                    wx = [v_sl, ("dve", max(v_w1ff, v_w2f))] if hc == 0 else []
                    if hc >= 4:
                        wx.append(("act", gels[hc - 4]))
                    for dc in range(8):
                        v_hmm = o.mm(pb[:, :],
                                     w1fb[:, dc * HID + hc * 128:dc * HID + hc * 128 + 128],
                                     slg[sl][:, dc * 512:(dc + 1) * 512],
                                     dc == 0, dc == 7,
                                     waits=wx if dc == 0 else (),
                                     inc=(dc == 7))
                    v_gel = o.actv(hTg[:, hc * 512:(hc + 1) * 512], pb[:, :],
                                   ACT.Gelu, waits=[("pe", v_hmm)], inc=True)
                    gels[hc] = v_gel
                for ni in range(4):
                    psl = ni % 2
                    for hc in range(32):
                        lhsT = hTg[:, hc * 512 + ni * 128:hc * 512 + ni * 128 + 128]
                        o.mm(PS[4 + 2 * psl][:, :], lhsT,
                             w2fb[:, hc * 1024:hc * 1024 + 512], hc == 0,
                             hc == 31,
                             waits=[("act", v_gel)] if hc == 0 else ())
                        v_pmm = o.mm(PS[5 + 2 * psl][:, :], lhsT,
                                     w2fb[:, hc * 1024 + 512:hc * 1024 + 1024],
                                     hc == 0, hc == 31, inc=(hc == 31))
                    v_pe2 = 0
                    for half in range(2):
                        v_pe2 = o.ts(pstb[psl][:, half * 512:half * 512 + 512],
                                     PS[4 + 2 * psl + half][:, :], 1.0,
                                     None, ALU.mult,
                                     waits=([("pe", v_pmm)] +
                                            ([v_pst_dma[psl]] if v_pst_dma[psl] else []))
                                     if half == 0 else (), inc=(half == 1))
                    v_pst_dma[psl] = o.dma_out(
                        p_d[b][g * 512 + ni * 128:g * 512 + ni * 128 + 128, :],
                        pstb[psl][:, :], [("dve", v_pe2)], ch=f"g{2 + psl}")
                v_hT_pe = o.v("pe")
            if b == 0:
                v_smlp_pe_b0 = o.v("pe")

        _snap_phase(o, 8)
        bw = o.barrier_waits()
        Pb = bfv(A, 0, 128, 32 * 1024)
        t4c = [bfv(C, 0, 128, 2048), bfv(C, 4096, 128, 2048)]
        fsl = [bfv(C, 8192, 128, 4096), bfv(C, 16384, 128, 4096)]
        u2t = bfv(C, 24576, 128, 4096)
        efb4 = bfv(C, 32768, 128, 2048)
        opc = [fv(C, 36864, 128, 1024), fv(C, 40960, 128, 1024)]
        ost = [fv(C, 45056, 128, 1024), fv(C, 49152, 128, 1024)]
        ftmp = fv(C, 53248, 128, 2048)
        s4row = fv(C, 0, 1, 2048)
        efrow = bfv(C, 61440, 1, 2048)
        for b in range(NB):
            v_P = 0
            for q in range(4):
                v_P = o.dma_in(
                    Pb[:].rearrange("p (c n) -> p c n", c=32)[:, q * 8:(q + 1) * 8, :],
                    hbm_blk(p_d[b])[:, q * 8:(q + 1) * 8, :],
                    ([("pe", o.v("pe"))] + (bw if b == 0 else [])) if q == 0 else (),
                    ch="q0")
            v_s4 = 0
            v_t4l_cons = [0, 0]
            for ncv in range(32):
                sl = ncv % 2
                wt4 = ([("pe", v_t4l_cons[sl])] if v_t4l_cons[sl] else []) + \
                    (bw if (b == 0 and ncv < 2) else [])
                v_tl = o.dma_in(t4c[sl][:, :],
                                hbm_blk(t4_d[b])[:, ncv:ncv + 1, :].squeeze(1),
                                wt4, ch=f"q{1 + sl}")
                for q in range(4):
                    v_s4 = o.mm(PS[q][0:1, :], onescb[:, :],
                                t4c[sl][:, q * 512:(q + 1) * 512],
                                ncv == 0, ncv == 31,
                                waits=[v_tl] if q == 0 else (),
                                inc=(q == 3))
                v_t4l_cons[sl] = v_s4
            v_s4r = 0
            for q in range(4):
                v_s4r = o.cp("dve", s4row[0:1, q * 512:(q + 1) * 512],
                             PS[q][0:1, :],
                             [("pe", v_s4)] if q == 0 else (), inc=(q == 3))
            d1 = 0
            if debug and b == 0:
                d1 = o.dma_out(cr_dbg[19:20, :], s4row[0:1, :],
                               [("dve", v_s4r)])
            v_ln = o.actv(s4row[0:1, :], s4row[0:1, :], ACT.Ln,
                          waits=[("dve", v_s4r)] +
                          ([d1] if d1 else []), inc=True)
            v_rt = o.ts(s4row[0:1, :], s4row[0:1, :], 4.0,
                        k0r[0:1, 2 * b:2 * b + 1], ALU.mult, ALU.add,
                        waits=[("act", v_ln)], inc=True)
            if debug and b == 0:
                o.dma_out(cr_dbg[20:21, :], s4row[0:1, :], [("dve", v_rt)])
                o.dma_out(cr_dbg[21:22, 0:2 * NB], k0r[0:1, :],
                          [("dve", v_rt)])
            v_ef = o.actv(efrow[0:1, :], s4row[0:1, :], ACT.Exp,
                          bias=k0r[0:1, 2 * b + 1:2 * b + 2], scale=-0.25,
                          waits=[("dve", v_rt)], inc=True)
            v_efb = 0
            for q in range(4):
                v_bc2 = o.mm(PS[4][:, :], onesrb[:, :],
                             efrow[0:1, q * 512:(q + 1) * 512], True, True,
                             waits=[("act", v_ef)] if q == 0 else
                             [("dve", v_efb)], inc=True)
                v_efb = o.cp("dve", efb4[:, q * 512:(q + 1) * 512], PS[4][:, :],
                             [("pe", v_bc2)], True)
            if debug and b == 0:
                o.dma_out(dbgm[:, :], efb4[:, 0:2048], [("dve", v_efb)])
            v_fsl_pe = [0, 0]
            v_ost_dma = [0, 0]
            v_ost_dve = [0, 0]
            v_cc2p = 0
            for scv in range(16):
                sl = scv % 2
                v_tl = o.dma_in(
                    fsl[sl][:].rearrange("p (c s) -> p c s", c=32),
                    hbm_blk(t4_d[b])[:, :, scv * 128:(scv + 1) * 128],
                    [("pe", v_fsl_pe[sl])] if v_fsl_pe[sl] else (),
                    ch=f"q{3 + sl}")
                v_ol = o.dma_in(opc[sl][:, :],
                                outp_d[(b * TCH + scv) * 128:(b * TCH + scv + 1) * 128, :],
                                [("dve", v_ost_dve[sl])] if v_ost_dve[sl] else (),
                                ch=f"q{5 + sl}")
                efvh = efb4[:, scv * 128:(scv + 1) * 128].unsqueeze(1) \
                    .broadcast_to([128, 16, 128])
                fview = fsl[sl][:]
                ftv = ftmp[:].rearrange("p (c s) -> p c s", c=16)
                for hf in range(2):
                    wf3 = [v_tl, ("dve", v_efb)]
                    if v_cc2p:
                        wf3.append(("act", v_cc2p))
                    v_f3 = o.tt(ftv, fsl[sl][:, hf * 2048:(hf + 1) * 2048]
                                .rearrange("p (c s) -> p c s", c=16), efvh,
                                ALU.mult, waits=wf3, inc=True)
                    v_cc2 = o.op("act", lambda e, s2=sl, h2=hf: e.copy(
                        out=fsl[s2][:, h2 * 2048:(h2 + 1) * 2048],
                        in_=ftmp[:, :]),
                        [("dve", v_f3)], "act")
                    v_cc2p = v_cc2
                o.tt(u2t[:], fview, fview, ALU.mult,
                     waits=[("act", v_cc2)])
                v_F = o.tt(fview, u2t[:], u2t[:], ALU.mult, inc=True)
                if debug and b == 0 and scv == 0:
                    o.dma_out(dbgh[:, :], fsl[0][:, 0:2048], [("dve", v_F)])
                p0 = 0 if sl == 0 else 4
                for ncv in range(32):
                    lhsT = fsl[sl][:, ncv * 128:(ncv + 1) * 128]
                    wx = [("dve", v_F), v_P] if ncv == 0 else ()
                    o.mm(PS[p0][:, :], lhsT,
                         Pb[:, ncv * 1024:ncv * 1024 + 512], ncv == 0,
                         ncv == 31, waits=wx)
                    o.mm(PS[p0 + 1][:, :], lhsT,
                         Pb[:, ncv * 1024 + 512:ncv * 1024 + 1024],
                         ncv == 0, ncv == 31)
                    v_cmm = o.mm(PS[p0 + 2][:, 0:1], lhsT, onescb[:, :],
                                 ncv == 0, ncv == 31, inc=(ncv == 31))
                v_fsl_pe[sl] = v_cmm
                v_ir = o.recip(smt[:, 10:11], PS[p0 + 2][:, 0:1],
                               waits=[("pe", v_cmm)], inc=True)
                v_ts2 = 0
                for half in range(2):
                    v_ts2 = o.ts(ost[sl][:, half * 512:half * 512 + 512],
                                 PS[p0 + half][:, :], smt[:, 10:11], None,
                                 ALU.mult,
                                 waits=([("dve", v_ir)] +
                                        ([v_ost_dma[sl]] if v_ost_dma[sl] else []))
                                 if half == 0 else (),
                                 inc=(half == 1 and debug and b == 0 and scv == 0))
                dq2 = 0
                if debug and b == 0 and scv == 0:
                    dq2 = o.dma_out(dbgq[:, 1024:2048], ost[0][:, 0:1024],
                                    [("dve", v_ts2)])
                v_o = o.tt(ost[sl][:, :], ost[sl][:, :], opc[sl][:, :],
                           ALU.add, waits=[v_ol] +
                           ([dq2] if dq2 else []), inc=True)
                v_ost_dve[sl] = v_o
                v_ost_dma[sl] = o.dma_out(
                    out_d[(b * TCH + scv) * 128:(b * TCH + scv + 1) * 128, :],
                    ost[sl][:, :], [("dve", v_o)], ch=f"g{sl}")

        if upto < 99 and upto in _PHASE_SNAPS:
            lens, cnts = _PHASE_SNAPS[upto]
            for e_ in ENGS:
                o.em[e_] = o.em[e_][:lens[e_]]
            o.cnt = dict(cnts)
        o.op("gp", lambda e: e.engine_nop(),
             waits=[(g, o.v(g)) for g in GCH if o.v(g)])

        with nc.Block() as block:
            @block.tensor
            def _(eng):
                for f in o.em["pe"]:
                    f(eng)

            @block.vector
            def _(eng):
                for f in o.em["dve"]:
                    f(eng)

            @block.scalar
            def _(eng):
                for f in o.em["act"]:
                    f(eng)

            @block.sync
            def _(eng):
                for f in o.em["sp"]:
                    f(eng)

            @block.gpsimd
            def _(eng):
                for f in o.em["gp"]:
                    f(eng)
    return nc


def _prep_inputs(inputs):
    w = inputs
    f32 = np.float32
    bf = ml_dtypes.bfloat16
    cm = {}
    cm["wqk"] = np.ascontiguousarray(np.concatenate(
        [w["L0_attn_W_Q"][:, :, 0].T, w["L0_attn_W_K"][:, :, 0].T],
        axis=1).astype(f32))
    cm["wv"] = np.ascontiguousarray(
        np.asarray(w["L0_attn_W_V"], f32).transpose(1, 0, 2).reshape(D, D))
    cm["wo"] = np.ascontiguousarray(
        np.asarray(w["L0_attn_W_O"], f32).transpose(2, 1, 0).reshape(D, D))
    cm["w1a"] = np.ascontiguousarray(np.asarray(w["L0_attn_W1"], f32))
    cm["w2a"] = np.ascontiguousarray(np.asarray(w["L0_attn_W2"], f32))
    cm["w1f"] = np.ascontiguousarray(np.asarray(w["L0_ffn_w1"], f32))
    cm["w2f"] = np.ascontiguousarray(np.asarray(w["L0_ffn_w2"], f32))
    cm["se"] = np.ascontiguousarray(
        np.asarray(w["L0_ffn_slot_embeds"], f32).reshape(NSL, D))
    vecs = np.zeros((128, 56), f32)
    for i, key in enumerate(["L0_attn_ln1_w", "L0_attn_ln1_b", "L0_attn_ln2_w",
                             "L0_attn_ln2_b", "L0_attn_b2", "L0_ffn_gamma_x",
                             "L0_ffn_gamma_slot"]):
        vecs[:, 8 * i:8 * i + 8] = np.asarray(w[key], f32).reshape(8, 128).T
    cm["vecs_c"] = vecs
    cm["b1a"] = np.ascontiguousarray(np.asarray(w["L0_attn_b1"], f32).reshape(1, HID))
    cm["b2row"] = np.ascontiguousarray(np.asarray(w["L0_attn_b2"], f32).reshape(1, D))
    cm["consts_c"] = np.ascontiguousarray(
        np.tile(np.asarray(CONSTS, f32)[None, :], (128, 1)))
    cm["identf_c"] = np.eye(128, dtype=f32)
    cm["identb_c"] = np.eye(128, dtype=f32).astype(bf)
    cm["trib_c"] = np.triu(np.ones((128, 128), f32)).astype(bf)
    cm["trif_c"] = np.triu(np.ones((128, 128), f32))
    ej = np.zeros((128, 31), f32)
    ej[:, 15] = 1.0
    cm["ejf_c"] = ej
    cm["ejb_c"] = ej.astype(bf)
    cm["u16f_c"] = np.triu(np.ones((16, 16), f32), 1)
    cm["onesrb_c"] = np.ones((1, 128), f32).astype(bf)
    cm["onesrf_c"] = np.ones((1, 128), f32)
    cm["onescb_c"] = np.ones((128, 1), f32).astype(bf)
    return cm


def kernel(**inputs):
    cm = _prep_inputs(inputs)
    x = np.asarray(inputs["L0_attn_x"], np.float32)
    nc = build(debug=False)
    in_maps = []
    for c in range(8):
        m = dict(cm)
        m["x"] = np.ascontiguousarray(x[c * NB:(c + 1) * NB].reshape(S, D))
        in_maps.append(m)
    res = run_bass_kernel_spmd(nc, in_maps, core_ids=list(range(8)))
    return np.concatenate(
        [r["out"].reshape(NB, T, D) for r in res.results], axis=0)
